# version 03
# baseline (speedup 1.0000x reference)
import math
import contextlib
import numpy as np
import ml_dtypes
import concourse.bass as bass
import concourse.mybir as mybir
from concourse.bass_utils import run_bass_kernel_spmd

F32 = mybir.dt.float32
BF16 = mybir.dt.bfloat16
AF = mybir.ActivationFunctionType
ALU = mybir.AluOpType
AX = mybir.AxisListType
BFDT = ml_dtypes.bfloat16

NH, HD, T, C, CT = 32, 128, 2048, 4096, 32
NTC, TCW = 4, 512
S128I = 1.0 / math.sqrt(128.0)
LNS128 = -math.log(math.sqrt(128.0))
NCORES = 8


class Sem:
    __slots__ = ("h", "n")

    def __init__(self, h):
        self.h = h
        self.n = 0


class Builder:
    ENGINES = ("tensor", "vector", "scalar", "gpsimd", "sync")

    def __init__(self, nc, es):
        self.nc = nc
        self.es = es
        self.ops = []
        self.sems = {}
        self.self_sems = {}
        self.nops = {e: 0 for e in self.ENGINES}

    def sem(self, name):
        if name not in self.sems:
            h = self.es.enter_context(self.nc.semaphore(name))
            self.sems[name] = Sem(h)
        return self.sems[name]

    @staticmethod
    def _flat(waits):
        out = []
        for w in waits:
            if w is None:
                continue
            if isinstance(w, list):
                out.extend(c for c in w if c is not None and c[1] > 0)
            elif w[1] > 0:
                out.append(w)
        return out

    def emit(self, eng, fn, waits=(), inc=None, inc_amt=1):
        waits = self._flat(waits)
        self_sem = self.self_sems.get(eng)
        if self_sem is not None:
            if self_sem.n > 0:
                waits = waits + [(self_sem, self_sem.n)]
            assert inc is None or inc is self_sem
            inc = self_sem
        incs = []
        if inc is not None:
            inc.n += inc_amt
            incs = [(inc, inc_amt)]
        self.ops.append((eng, fn, waits, incs))
        self.nops[eng] += 1

    def dma(self, queue, out_ap, in_ap, waits=(), inc=None, **kw):
        self.emit(queue, lambda e: e.dma_start(out_ap, in_ap, **kw),
                  waits, inc, inc_amt=16)

    def build_block(self):
        nc = self.nc
        with nc.Block() as block:
            for eng_name in self.ENGINES:
                ops = [o for o in self.ops if o[0] == eng_name]
                if not ops:
                    continue

                def body(e, ops=ops):
                    lw = {}
                    for (_, fn, waits, incs) in ops:
                        for (s, v) in waits:
                            if lw.get(id(s), 0) < v:
                                e.wait_ge(s.h, v)
                                lw[id(s)] = v
                        instr = fn(e)
                        for (s, amt) in incs:
                            instr.then_inc(s.h, amt)

                getattr(block, eng_name)(body)


def build_program(stop_after=None):
    nc = bass.Bass("TRN2", target_bir_lowering=False)
    dt = nc.dram_tensor

    xt = dt("xt", [CT, 128, T], F32, kind="ExternalInput").ap()
    wq = dt("wq", [NH, 2, 128, 16, 128], F32, kind="ExternalInput").ap()
    wk = dt("wk", [NH, 2, 128, 16, 128], F32, kind="ExternalInput").ap()
    wv = dt("wv", [NH, 2, 128, 16, 128], F32, kind="ExternalInput").ap()
    wo = dt("wo", [NH, 2, 128, 16, 128], F32, kind="ExternalInput").ap()
    bw = dt("bw", [NH, 2, 128, 16, 128], F32, kind="ExternalInput").ap()
    gwm = dt("gwm", [128, 256], F32, kind="ExternalInput").ap()
    acm = dt("acm", [128, 4096], F32, kind="ExternalInput").ap()
    bcm = dt("bcm", [128, 4096], F32, kind="ExternalInput").ap()
    bbm = dt("bbm", [128, 32], F32, kind="ExternalInput").ap()
    cost = dt("cost", [128, T], BF16, kind="ExternalInput").ap()
    sint = dt("sint", [128, T], BF16, kind="ExternalInput").ap()
    maskm = dt("maskm", [128, 128], BF16, kind="ExternalInput").ap()
    rallm = dt("rallm", [128, 2048], BF16, kind="ExternalInput").ap()
    idm = dt("idm", [128, 128], F32, kind="ExternalInput").ap()
    outm = dt("outm", [CT, 128, T], F32, kind="ExternalOutput").ap()

    qs = dt("qs", [NH, 128, T], BF16, kind="Internal").ap()
    ks = dt("ks", [NH, 128, T], BF16, kind="Internal").ap()
    vs = dt("vs", [NH, 128, T], BF16, kind="Internal").ap()
    r1s = dt("r1s", [CT, 128, T], F32, kind="Internal").ap()
    x1s = dt("x1s", [CT, 128, T], F32, kind="Internal").ap()
    r2s = dt("r2s", [CT, 128, T], F32, kind="Internal").ap()

    es = contextlib.ExitStack()
    with es:
        E = es.enter_context
        st = nc.sbuf_tensor
        big1 = E(st("big1", [128, 65536], BF16))
        wring = E(st("wring", [128, 6144], BF16))
        tabs = E(st("tabs", [128, 4096], BF16))
        qkb = E(st("qkb", [128, 5120], BF16))
        vb3 = E(st("vb3", [128, 4096], BF16))
        preb = E(st("preb", [128, 4096], BF16))
        sqa = E(st("sqa", [128, 4096], BF16))
        f32a = E(st("f32a", [128, 3328], F32))
        rows = E(st("rows", [128, 2048], F32))
        bbs = E(st("bbs", [128, 32], F32))
        gws = E(st("gws", [128, 256], BF16))
        ident = E(st("ident", [128, 128], F32))
        maskb = E(st("maskb", [128, 128], BF16))
        onesf = E(st("onesf", [128, 256], F32))
        onesb = E(st("onesb", [128, 1], BF16))
        crow = E(st("crow", [128, 8], F32))
        pb = [E(nc.psum_tensor(f"pb{i}", [128, 512], F32)) for i in range(8)]

        b = Builder(nc, es)

        wslot = lambda s: wring[:, s * 2048:(s + 1) * 2048]
        costab = tabs[:, 0:2048]
        sintab = tabs[:, 2048:4096]
        qrope, krope = qkb[:, 0:2048], qkb[:, 2048:4096]
        vbA = lambda p_: vb3[:, p_ * 2048:(p_ + 1) * 2048]
        pre = lambda p_: preb[:, p_ * 2048:(p_ + 1) * 2048]
        sqbuf = sqa[:, 0:2048]
        ropetmp = sqa[:, 2048:4096]
        qparts = rows[64:65, 1568:1696]
        qm2row = rows[64:65, 1536:1568]
        knm2row = rows[64:65, 0:512]
        b2row = rows[64:65, 512:1024]
        lntmp = rows[64:65, 1024:1536]
        zrecr = rows[64:65, 1536:2048]
        browf = rows[64:65, 0:512]
        arow, brow = rows[0:1, 0:2048], rows[32:33, 0:2048]
        krsB = lambda p_: qkb[:, p_ * 2048:(p_ + 1) * 2048]
        qchB = lambda p_: qkb[:, 4096 + p_ * 512:4096 + (p_ + 1) * 512]
        vtok = lambda p_: vb3[:, p_ * 2048:(p_ + 1) * 2048]
        _attn_off = (0, 512, 1024, 2048)
        attnS = lambda s: sqa[:, _attn_off[s]:_attn_off[s] + 512]
        ebB = sqa[:, 1536:2048]
        negb = f32a[:, 0:512]
        ebf = f32a[:, 512:1024]
        zbc = f32a[:, 1024:1536]
        xres = lambda p_: f32a[:, p_ * 512:(p_ + 1) * 512]
        rtile = lambda p_: f32a[:, 1024 + p_ * 512:1024 + (p_ + 1) * 512]
        rload = lambda p_: f32a[:, p_ * 512:(p_ + 1) * 512]
        abc, bbc = f32a[:, 1024:1536], f32a[:, 1536:2048]
        outt = lambda p_: f32a[:, 2048 + p_ * 512:2048 + (p_ + 1) * 512]
        gatef = f32a[:, 3072:3104]
        eq1v, eq2v, l2v = f32a[:, 3104:3112], f32a[:, 3112:3120], f32a[:, 3120:3128]
        wcatf = f32a[:, 3200:3328]
        rbB = sqa[:, 0:512]
        sqB = sqa[:, 512:1024]
        wcatTb = sqa[:, 1280:1408]
        rall = sqa[:, 2048:4096]
        lowTb = preb[:, 0:2048]
        tmpT = preb[:, 2048:4096]
        wexpb = tabs[:, 0:2048]
        acatS = vb3[:, 0:4096]
        bcatS = qkb[:, 0:4096]

        pe = b.sem("pe")
        acte = b.sem("acte")
        dve = b.sem("dve")
        bar = b.sem("bar")
        b.self_sems["vector"] = dve
        b.self_sems["scalar"] = acte
        gconst = b.sem("gconst")
        gx = b.sem("gx")
        gw3 = [b.sem(f"gw{i}") for i in range(3)]
        gk2 = [b.sem(f"gk{i}") for i in range(2)]
        gq2 = [b.sem(f"gq{i}") for i in range(2)]
        gxr2 = [b.sem(f"gxr{i}") for i in range(2)]
        grl2 = [b.sem(f"grl{i}") for i in range(2)]
        gd = b.sem("gd")
        sqs = b.sem("sqs")
        sks = b.sem("sks")
        sv2 = [b.sem(f"sv{i}") for i in range(2)]
        vt2 = [b.sem(f"vt{i}") for i in range(2)]
        sr2 = [b.sem(f"sr{i}") for i in range(2)]
        so2 = [b.sem(f"so{i}") for i in range(2)]
        ALL_DMA = [gconst, gx, gd, sqs, sks] + gw3 + gk2 + gq2 + gxr2 + grl2 + sv2 + vt2 + sr2 + so2

        def dma_totals():
            return [(sm, sm.n) for sm in ALL_DMA]

        for (dst, src) in ((costab, cost[:]), (sintab, sint[:]),
                           (maskb[:], maskm[:]), (ident[:], idm[:]),
                           (bbs[:], bbm[:]), (gws[:], gwm[:])):
            b.dma("gpsimd", dst, src, inc=gconst)
        CONSTS = (gconst, gconst.n)
        b.emit("vector", lambda e: e.memset(onesf[:, 0:128], 1.0))
        b.emit("vector", lambda e: e.memset(onesf[:, 128:256], -1.0))
        b.emit("vector", lambda e: e.memset(crow[64:65, 0:1], LNS128))
        b.emit("vector", lambda e: e.memset(crow[64:65, 1:2], 1e-5))
        b.emit("vector", lambda e: e.memset(onesb[:], 1.0), inc=dve)
        CONST_DVE = dve.n


        def early_stop():
            b.dma("sync", outm[0], rows[:, :], inc=so2[0])
            b.emit("sync", lambda e: e.nop(),
                   waits=dma_totals() + [(dve, dve.n), (acte, acte.n),
                                         (pe, pe.n)], inc=bar)
            b.build_block()
        for ct in range(CT):
            b.dma("gpsimd", big1[:, ct * 2048:(ct + 1) * 2048], xt[ct], inc=gx)
        XLD = (gx, gx.n)

        wsrc = ("q", "k", "v")
        wmap = {"q": wq, "k": wk, "v": wv}
        half_gld = {}
        slot_free = {0: None, 1: None, 2: None}
        m_next = 0

        def emit_half_load(m):
            nonlocal m_next
            p_, hf = divmod(m, 2)
            h, wi = divmod(p_, 3)
            sm = gw3[m % 3]
            b.dma("gpsimd", wslot(m % 3), wmap[wsrc[wi]][h, hf],
                  waits=[slot_free[m % 3]], inc=sm)
            half_gld[m] = (sm, sm.n)
            m_next = m + 1

        NPROJ = 3 * NH
        skip_bprep = False
        if stop_after and stop_after.startswith("A:"):
            NPROJ = int(stop_after.split(":")[1])
            skip_bprep = True
        if NPROJ >= 0:
            for m in range(3):
                emit_half_load(m)

        group_free = {0: [], 1: []}
        spill_q = None
        spill_k = None
        vb_free = {0: None, 1: None}
        pre_free = {0: None, 1: None}
        penorm = b.sem("penorm")
        pending_norm = []

        for p in range(max(0, NPROJ)):
            h, wi = divmod(p, 3)
            wname = wsrc[wi]
            G = p % 2
            banks = pb[4 * G:4 * G + 4]
            gf = []
            for it in group_free[G]:
                if isinstance(it, tuple) and it[0] == "norm_red":
                    gf.extend(it[1])
                else:
                    gf.append(it)
            waits0 = [half_gld[2 * p], half_gld[2 * p + 1], XLD] + gf
            for ct in range(CT):
                hf, cth = divmod(ct, 16)
                lhsT = wslot((2 * p + hf) % 3)[:, cth * 128:(cth + 1) * 128]
                for tc in range(NTC):
                    rhs = big1[:, ct * 2048 + tc * TCW:ct * 2048 + (tc + 1) * TCW]
                    last = (ct == CT - 1)
                    b.emit("tensor",
                           (lambda e, o=banks[tc][:, :], l=lhsT, r=rhs,
                            s=(ct == 0), p2=last:
                            e.matmul(o, l, r, start=s, stop=p2)),
                           waits=(waits0 if (ct == 0 and tc == 0) else ()),
                           inc=(pe if (last and tc == NTC - 1) else None))
            proj_done = pe.n
            slot_free[(2 * p) % 3] = (pe, proj_done)
            slot_free[(2 * p + 1) % 3] = (pe, proj_done)
            if m_next < 192:
                emit_half_load(m_next)
            if m_next < 192:
                emit_half_load(m_next)
            while pending_norm:
                pending_norm.pop(0)()

            if wname == "v":
                dest = vbA(h % 2)
                evw = [(pe, proj_done), vb_free[h % 2]]
            else:
                dest = pre(p % 2)
                evw = [(pe, proj_done), pre_free[p % 2]]
            for tc in range(NTC):
                b.emit("scalar",
                       (lambda e, o=dest[:, tc * TCW:(tc + 1) * TCW],
                        i=banks[tc][:, :]: e.copy(o, i)),
                       waits=(evw if tc == 0 else ()),
                       inc=(acte if tc == NTC - 1 else None))
            evac_done = acte.n

            if wname == "v":
                svm = sv2[h % 2]
                b.dma("sync", vs[h], dest, waits=[(acte, evac_done)], inc=svm)
                vb_free[h % 2] = (svm, svm.n)
                group_free[G] = [(acte, evac_done)]
            else:
                b.emit("vector",
                       (lambda e, d=dest: e.tensor_mul(sqbuf[:], d, d)),
                       waits=[(acte, evac_done), (penorm, penorm.n + 4 * len(pending_norm))],
                       inc=dve)
                sq_done = dve.n
                red_box = []
                def _emit_norms(banks=banks, sq_done=sq_done, wname=wname, h=h,
                                red_box=red_box):
                    for tc in range(NTC):
                        nb = banks[tc]
                        w = [(dve, sq_done)]
                        b.emit("tensor",
                               (lambda e, o=nb[0:1, :],
                                r=sqbuf[:, tc * TCW:(tc + 1) * TCW]:
                                e.matmul(o, onesb[:], r, start=True, stop=True)),
                               waits=w, inc=penorm)
                        mm_done = penorm.n
                        if wname == "q":
                            b.emit("vector",
                                   (lambda e, o=qparts[0:1, h * 4 + tc:h * 4 + tc + 1],
                                    i=nb[0:1, :]: e.reduce_max(o, i, axis=AX.X)),
                                   waits=[(penorm, mm_done)], inc=dve)
                        else:
                            b.emit("vector",
                                   (lambda e,
                                    o=knm2row[0:1, h * 16 + tc * 4:h * 16 + tc * 4 + 4],
                                    i=nb[0:1, :].rearrange("p (a x) -> p a x", a=4):
                                    e.reduce_max(o, i, axis=AX.X)),
                                   waits=[(penorm, mm_done)], inc=dve)
                    red_box.append((dve, dve.n))
                pending_norm.append(_emit_norms)
                norms_done = (penorm, penorm.n + 4 * len(pending_norm))
                rope_out = qrope if wname == "q" else krope
                rope_war = spill_q if wname == "q" else spill_k
                b.emit("vector",
                       (lambda e, o=rope_out, d=dest: e.tensor_mul(o, d, costab)),
                       waits=[rope_war, CONSTS])
                b.emit("vector",
                       (lambda e, d=dest: e.tensor_mul(ropetmp[:], d, sintab)))
                b.emit("vector",
                       (lambda e: e.tensor_copy(sqbuf[0:64, :], ropetmp[64:128, :])),
                       waits=[norms_done])
                b.emit("vector",
                       (lambda e: e.tensor_copy(sqbuf[64:128, :], ropetmp[0:64, :])))
                b.emit("vector",
                       (lambda e, o=rope_out: e.tensor_add(o, o, sqbuf[:])),
                       inc=dve)
                rope_done = dve.n
                pre_free[p % 2] = (dve, rope_done)
                ssm = sqs if wname == "q" else sks
                b.dma("sync", (qs if wname == "q" else ks)[h], rope_out,
                      waits=[(dve, rope_done)], inc=ssm)
                if wname == "q":
                    spill_q = (ssm, ssm.n)
                else:
                    spill_k = (ssm, ssm.n)
                group_free[G] = [(acte, evac_done), ("norm_red", red_box)]

        while pending_norm:
            pending_norm.pop(0)()
        if skip_bprep:
            early_stop()
            return nc, b.nops
        b.emit("vector",
               (lambda e: e.reduce_max(
                   qm2row[0:1, 0:32],
                   qparts.rearrange("p (h f) -> p h f", f=4), axis=AX.X)))
        for h in range(NH):
            b.emit("vector",
                   (lambda e, o=b2row[0:1, h * 16:(h + 1) * 16],
                    i=knm2row[0:1, h * 16:(h + 1) * 16],
                    s=qm2row[0:1, h:h + 1]:
                    e.tensor_scalar(o, i, s, None, ALU.mult)),
                   inc=(dve if h == NH - 1 else None))
        BPREP = dve.n
        if stop_after == "AB1":
            early_stop()
            return nc, b.nops
        b.emit("scalar",
               lambda e: e.activation(lntmp[0:1, :], b2row[0:1, :], AF.Ln),
               waits=[(dve, BPREP)])
        b.emit("scalar",
               lambda e: e.activation(browf[0:1, :], lntmp[0:1, :], AF.Exp,
                                      scale=0.5, bias=crow[64:65, 0:1]),
               inc=acte)
        BROW = acte.n
        if stop_after == "AB2":
            early_stop()
            return nc, b.nops
        b.emit("tensor",
               lambda e: e.matmul(pb[0][:, :], onesf[64:65, 128:256], browf[0:1, :],
                                  start=True, stop=True),
               waits=[(acte, BROW), (dve, CONST_DVE)], inc=pe)
        BB_PE = pe.n
        if stop_after == "AB2a":
            early_stop()
            return nc, b.nops
        b.emit("scalar", lambda e: e.copy(negb[:], pb[0][:, :]),
               waits=[(pe, BB_PE)])
        b.emit("scalar", lambda e: e.activation(ebf[:], pb[0][:, :], AF.Exp,
                                                scale=-1.0),
               waits=[(pe, BB_PE)])
        EB_ACT = acte.n
        if stop_after == "AB2c":
            early_stop()
            return nc, b.nops
        b.emit("vector", lambda e: e.tensor_copy(ebB[:], ebf[:]),
               waits=[(acte, EB_ACT), (sqs, sqs.n), (sks, sks.n)], inc=dve)
        EBB = dve.n
        if stop_after == "AB3":
            early_stop()
            return nc, b.nops

        b.emit("sync", lambda e: e.nop(),
               waits=dma_totals() + [(dve, dve.n), (acte, acte.n), (pe, pe.n)],
               inc=bar)
        BAR_A = bar.n
        if stop_after == "A":
            early_stop()
            return nc, b.nops

        pesc = b.sem("pesc")
        peav = b.sem("peav")
        poolm = b.sem("poolm")
        exp_vals = []
        fin_val = [0, 0]
        krs_free = {0: None, 1: None}
        qch_free = {0: None, 1: None}
        vtok_free = {0: None, 1: None}
        attn_free = [None, None, None, None]
        blk_i = 0

        for h in range(NH):
            par = h % 2
            b.dma("gpsimd", krsB(par), ks[h],
                  waits=[(bar, BAR_A), krs_free[par]], inc=gk2[par])
            k_ld = (gk2[par], gk2[par].n)
            for j in range(16):
                b.dma("sync", vtok(par)[:, j * 128:(j + 1) * 128],
                      vs[h][:, j * 128:(j + 1) * 128],
                      waits=([(bar, BAR_A), vtok_free[par]] if j == 0 else ()),
                      inc=vt2[par], transpose=True)
            vt_all = (vt2[par], vt2[par].n)
            for j in range(16):
                b.emit("vector",
                       (lambda e, o=vtok(par)[:, j * 128:(j + 1) * 128],
                        s=ebf[:, h * 16 + j:h * 16 + j + 1]:
                        e.tensor_scalar(o, o, s, None, ALU.mult)),
                       waits=[vt_all, (acte, EB_ACT), (dve, EBB)])
            pres_done = dve.n

            for c in range(NTC):
                qpar = c % 2
                b.dma("gpsimd", qchB(qpar), qs[h][:, c * TCW:(c + 1) * TCW],
                      waits=[(bar, BAR_A), qch_free[qpar]], inc=gq2[qpar])
                q_ld = (gq2[qpar], gq2[qpar].n)
                J = 4 * c + 4
                avb = pb[2 if (h * NTC + c) % 2 == 0 else 5]
                zb_ = pb[3 if (h * NTC + c) % 2 == 0 else 6]
                for j in range(J):
                    off = max(0, j * 128 - c * TCW)
                    scb = pb[(0, 1, 7)[blk_i % 3]]
                    slot = blk_i % 4
                    b.emit("tensor",
                           (lambda e, o=scb[:, off:TCW],
                            l=krsB(par)[:, j * 128:(j + 1) * 128],
                            r=qchB(qpar)[:, off:TCW]:
                            e.matmul(o, l, r, start=True, stop=True)),
                           waits=[k_ld, q_ld,
                                  ((acte, exp_vals[blk_i - 3])
                                   if blk_i >= 3 else None)],
                           inc=pesc)
                    sc_done = pesc.n
                    b.emit("scalar",
                           (lambda e, o=attnS(slot)[:, off:TCW],
                            i=scb[:, off:TCW],
                            bia=negb[:, h * 16 + j:h * 16 + j + 1]:
                            e.activation(o, i, AF.Exp, bias=bia, scale=S128I)),
                           waits=[(pesc, sc_done), attn_free[slot]])
                    exp_done = acte.n
                    exp_vals.append(exp_done)
                    mask_done = None
                    if j >= 4 * c:
                        b.emit("gpsimd",
                               (lambda e, o=attnS(slot)[:, off:off + 128]:
                                e.tensor_mul(o, o, maskb[:])),
                               waits=[(acte, exp_done)], inc=poolm)
                        mask_done = (poolm, poolm.n)
                    avw = [(acte, exp_done), (dve, pres_done), mask_done]
                    if j == 0:
                        avw.append((dve, fin_val[(h * NTC + c) % 2]))
                    b.emit("tensor",
                           (lambda e, o=avb[:, off:TCW],
                            l=vtok(par)[:, j * 128:(j + 1) * 128],
                            r=attnS(slot)[:, off:TCW], s=(j == 0),
                            p2=(j == J - 1):
                            e.matmul(o, l, r, start=s, stop=p2)),
                           waits=avw)
                    b.emit("tensor",
                           (lambda e, o=zb_[0:1, off:TCW],
                            l=ebB[:, h * 16 + j:h * 16 + j + 1],
                            r=attnS(slot)[:, off:TCW], s=(j == 0),
                            p2=(j == J - 1):
                            e.matmul(o, l, r, start=s, stop=p2)),
                           inc=peav)
                    attn_free[slot] = (peav, peav.n)
                    blk_i += 1
                avz_done = peav.n
                b.emit("vector",
                       (lambda e, i=zb_[0:1, :]: e.reciprocal(zrecr[0:1, :], i)),
                       waits=[(peav, avz_done)])
                rec = dve.n
                b.emit("tensor",
                       (lambda e: e.matmul(pb[4][:, :], onesf[64:65, 0:128], zrecr[0:1, :],
                                           start=True, stop=True)),
                       waits=[(dve, rec)], inc=pe)
                zb_pe = pe.n
                b.emit("scalar", (lambda e: e.copy(zbc[:], pb[4][:, :])),
                       waits=[(pe, zb_pe)])
                zbc_done = acte.n
                b.emit("vector",
                       (lambda e, i=avb[:, :],
                        o=big1[:, h * 2048 + c * TCW:h * 2048 + (c + 1) * TCW]:
                        e.tensor_mul(o, i, zbc[:])),
                       waits=[(acte, zbc_done)])
                fin_val[(h * NTC + c) % 2] = dve.n
                qch_free[qpar] = (pesc, pesc.n)
            krs_free[par] = (pesc, pesc.n)
            vtok_free[par] = (peav, peav.n)

        bank_free = {}

        def proj_phase(wt, rhs_of, resid_list, spill_list, with_bias_lora):
            half_gld2 = {}
            slot_free2 = {0: None, 1: None, 2: None}
            stat_rows_done = None
            xres_free = [None, None]
            rtile_free = [None, None]
            rb_free = None
            pending_stats = []
            pestat = b.sem("pestat")

            def load_half(m):
                obr = (m // 2) % 32
                hf = m % 2
                sm = gw3[m % 3]
                b.dma("gpsimd", wslot(m % 3), wt[obr, hf],
                      waits=[(bar, bar.n), slot_free2[m % 3]], inc=sm)
                half_gld2[m] = (sm, sm.n)

            for mm_ in range(3):
                load_half(mm_)
            m2 = 3
            for pair in range(2):
                tcs = (2 * pair, 2 * pair + 1)
                for ob in range(CT):
                    m = (pair * 32 + ob) * 2
                    last_tc_pe = None
                    for k_tc, tc in enumerate(tcs):
                        bank = pb[(0 if ob % 2 == 0 else 6) + k_tc]
                        w0 = [half_gld2[m], half_gld2[m + 1], (bar, bar.n),
                              bank_free.get((ob % 2, k_tc))]
                        for ct in range(CT):
                            hf, cth = divmod(ct, 16)
                            lhsT = wslot((m + hf) % 3)[:, cth * 128:(cth + 1) * 128]
                            rhs = rhs_of(ct, tc)
                            last = (ct == CT - 1 and not with_bias_lora)
                            b.emit("tensor",
                                   (lambda e, o=bank[:, :], l=lhsT, r=rhs,
                                    s=(ct == 0), p2=last:
                                    e.matmul(o, l, r, start=s, stop=p2)),
                                   waits=(w0 if ct == 0 else ()),
                                   inc=(pe if last else None))
                        if with_bias_lora:
                            b.emit("tensor",
                                   (lambda e, o=bank[:, :],
                                    l=bcatS[:, ob * 128:(ob + 1) * 128],
                                    r=tmpT[:, tc * TCW:(tc + 1) * TCW]:
                                    e.matmul(o, l, r, start=False, stop=True)),
                                   inc=pe)
                        proj_done = pe.n
                        last_tc_pe = proj_done
                        xr = ob % 2
                        b.dma("gpsimd", xres(xr),
                              resid_list[ob][:, tc * TCW:(tc + 1) * TCW],
                              waits=[xres_free[xr], (bar, bar.n)], inc=gxr2[xr])
                        xres_ld = (gxr2[xr], gxr2[xr].n)
                        rt = (ob * 2 + k_tc) % 2
                        b.emit("vector",
                               (lambda e, o=rtile(rt), i=bank[:, :],
                                x=xres(xr): e.tensor_add(o, i, x)),
                               waits=[(pe, proj_done), xres_ld,
                                      rtile_free[rt]],
                               inc=(None if with_bias_lora else dve))
                        if with_bias_lora:
                            b.emit("vector",
                                   (lambda e, o=rtile(rt), bia=bbs[:, ob:ob + 1]:
                                    e.tensor_scalar(o, o, bia, None, ALU.add)),
                                   inc=dve)
                        radd = dve.n
                        xres_free[xr] = (dve, radd)
                        bank_free[(ob % 2, k_tc)] = (dve, radd)
                        srm = sr2[rt]
                        b.dma("sync", spill_list[ob][:, tc * TCW:(tc + 1) * TCW],
                              rtile(rt), waits=[(dve, radd)], inc=srm)
                        spill_done = (srm, srm.n)
                        b.emit("scalar",
                               (lambda e, i=rtile(rt): e.copy(rbB[:], i)),
                               waits=[(dve, radd), rb_free], inc=acte)
                        rb_done = acte.n
                        rtile_free[rt] = [spill_done, (acte, rb_done)]
                        b.emit("vector",
                               (lambda e: e.tensor_mul(sqB[:], rbB[:], rbB[:])),
                               waits=[(acte, rb_done), rb_free], inc=dve)
                        sq_done = dve.n
                        def _emit_stats(k_tc=k_tc, ob=ob, rb_done=rb_done,
                                        sq_done=sq_done, srd=stat_rows_done):
                            sumw = [(acte, rb_done)]
                            if ob == 0:
                                sumw.append(srd)
                            b.emit("tensor",
                                   (lambda e, o=pb[2 + k_tc][0:1, :], s=(ob == 0),
                                    p2=(ob == CT - 1):
                                    e.matmul(o, onesb[:], rbB[:], start=s, stop=p2)),
                                   waits=sumw)
                            b.emit("tensor",
                                   (lambda e, o=pb[4 + k_tc][0:1, :], s=(ob == 0),
                                    p2=(ob == CT - 1):
                                    e.matmul(o, onesb[:], sqB[:], start=s, stop=p2)),
                                   waits=[(dve, sq_done)], inc=pestat)
                        pending_stats.append(_emit_stats)
                        rb_free = (pestat, pestat.n + len(pending_stats))
                    slot_free2[m % 3] = (pe, last_tc_pe)
                    while len(pending_stats) > 2:
                        pending_stats.pop(0)()
                    slot_free2[(m + 1) % 3] = (pe, last_tc_pe)
                    if m2 < 128:
                        load_half(m2)
                        m2 += 1
                    if m2 < 128:
                        load_half(m2)
                        m2 += 1
                while pending_stats:
                    pending_stats.pop(0)()
                stat_pe = pe.n
                stat_done = (pestat, pestat.n)
                rstd64 = knm2row
                for k_tc, tcx in enumerate(tcs):
                    tc = tcx
                    b.emit("vector",
                           (lambda e, i=pb[2 + k_tc][0:1, :]: e.tensor_scalar(
                               lntmp[0:1, :], i, 1.0 / C, None, ALU.mult)),
                           waits=[(pe, stat_pe), stat_done])
                    b.emit("vector",
                           (lambda e, i=pb[4 + k_tc][0:1, :]: e.tensor_scalar(
                               zrecr[0:1, :], i, 1.0 / C, None, ALU.mult)))
                    b.emit("vector",
                           (lambda e: e.tensor_mul(b2row[0:1, :], lntmp[0:1, :],
                                                   lntmp[0:1, :])))
                    b.emit("vector",
                           (lambda e: e.tensor_sub(zrecr[0:1, :], zrecr[0:1, :],
                                                   b2row[0:1, :])),
                           inc=dve)
                    var_done = dve.n
                    b.emit("scalar",
                           (lambda e: e.activation(b2row[0:1, :], zrecr[0:1, :],
                                                   AF.Ln, bias=crow[64:65, 1:2])),
                           waits=[(dve, var_done)])
                    b.emit("scalar",
                           (lambda e: e.activation(rstd64[0:1, :], b2row[0:1, :],
                                                   AF.Exp, scale=-0.5)),
                           inc=acte)
                    rstd = acte.n
                    b.emit("vector",
                           (lambda e: e.tensor_mul(b2row[0:1, :], lntmp[0:1, :],
                                                   rstd64[0:1, :])),
                           waits=[(acte, rstd)])
                    b.emit("vector",
                           (lambda e, o=arow[0:1, tc * TCW:(tc + 1) * TCW]:
                            e.tensor_copy(o, rstd64[0:1, :])))
                    b.emit("vector",
                           (lambda e, o=brow[0:1, tc * TCW:(tc + 1) * TCW]:
                            e.tensor_scalar(o, b2row[0:1, :], -1.0, None,
                                            ALU.mult)),
                           inc=dve)
                stat_rows_done = (dve, dve.n)

        def apply_phase(r_list, out_list, to_big1):
            rl_free = [None, None]
            ot_free = [None, None]
            for tc in range(NTC):
                b.emit("tensor",
                       (lambda e, r=arow[0:1, tc * TCW:(tc + 1) * TCW]:
                        e.matmul(pb[4][:, :], onesf[0:1, 0:128], r, start=True, stop=True)),
                       waits=[(bar, bar.n), (acte, acte.n), (dve, dve.n)])
                b.emit("tensor",
                       (lambda e, r=brow[0:1, tc * TCW:(tc + 1) * TCW]:
                        e.matmul(pb[5][:, :], onesf[32:33, 0:128], r, start=True, stop=True)),
                       inc=pe)
                bc_pe = pe.n
                b.emit("scalar", lambda e: e.copy(abc[:], pb[4][:, :]),
                       waits=[(pe, bc_pe)])
                b.emit("scalar", lambda e: e.copy(bbc[:], pb[5][:, :]), inc=acte)
                bc_act = acte.n
                for ob in range(CT):
                    rp = ob % 2
                    b.dma("gpsimd", rload(rp),
                          r_list[ob][:, tc * TCW:(tc + 1) * TCW],
                          waits=[rl_free[rp], (bar, bar.n)], inc=grl2[rp])
                    r_ld = (grl2[rp], grl2[rp].n)
                    op = ob % 2
                    b.emit("vector",
                           (lambda e, o=outt(op), r=rload(rp):
                            e.tensor_mul(o, r, abc[:])),
                           waits=[r_ld, (acte, bc_act), ot_free[op]])
                    b.emit("vector",
                           (lambda e, o=outt(op): e.tensor_add(o, o, bbc[:])),
                           inc=dve)
                    ap_done = dve.n
                    rl_free[rp] = (dve, ap_done)
                    som = so2[op]
                    b.dma("sync", out_list[ob][:, tc * TCW:(tc + 1) * TCW],
                          outt(op), waits=[(dve, ap_done)], inc=som)
                    conds = [(som, som.n)]
                    if to_big1:
                        b.emit("scalar",
                               (lambda e,
                                o=big1[:, ob * 2048 + tc * TCW:
                                       ob * 2048 + (tc + 1) * TCW],
                                i=outt(op): e.copy(o, i)),
                               waits=[(dve, ap_done)], inc=acte)
                        conds.append((acte, acte.n))
                    ot_free[op] = conds

        if stop_after == "B":
            early_stop()
            return nc, b.nops
        b.emit("sync", lambda e: e.nop(),
               waits=dma_totals() + [(pe, pe.n), (pesc, pesc.n), (peav, peav.n),
                                     (acte, acte.n), (dve, dve.n)], inc=bar)
        avT_rhs = lambda ct, tc: big1[:, ct * 2048 + tc * TCW:ct * 2048 + (tc + 1) * TCW]
        xt_list = [xt[i] for i in range(CT)]
        r1_list = [r1s[i] for i in range(CT)]
        proj_phase(wo, avT_rhs, xt_list, r1_list, False)

        if stop_after == "C1":
            early_stop()
            return nc, b.nops
        b.emit("sync", lambda e: e.nop(),
               waits=dma_totals() + [(dve, dve.n), (pe, pe.n), (acte, acte.n)],
               inc=bar)
        x1_list = [x1s[i] for i in range(CT)]
        apply_phase(r1_list, x1_list, True)

        if stop_after == "C2":
            early_stop()
            return nc, b.nops
        b.emit("sync", lambda e: e.nop(),
               waits=dma_totals() + [(dve, dve.n), (pe, pe.n), (acte, acte.n)],
               inc=bar)
        BAR_C2 = bar.n
        b.dma("gpsimd", acatS, acm[:], waits=[(bar, BAR_C2)], inc=gd)
        b.dma("gpsimd", bcatS, bcm[:], waits=[(bar, BAR_C2)], inc=gd)
        b.dma("gpsimd", rall[:], rallm[:], waits=[(bar, BAR_C2)], inc=gd)
        D_LD = (gd, gd.n)

        gate_prev = None
        for tt in range(16):
            gbank = pb[6 + tt % 2]
            for ct in range(CT):
                b.emit("tensor",
                       (lambda e, o=gbank[:, 0:8],
                        l=big1[:, ct * 2048 + tt * 128:ct * 2048 + (tt + 1) * 128],
                        r=gws[:, ct * 8:(ct + 1) * 8], s=(ct == 0),
                        p2=(ct == CT - 1):
                        e.matmul(o, l, r, start=s, stop=p2)),
                       waits=([(bar, BAR_C2), gate_prev] if ct == 0 else ()),
                       inc=(pe if ct == CT - 1 else None))
            g_pe = pe.n
            b.emit("vector",
                   (lambda e, i=gbank[:, 0:8]: e.reduce_max(
                       gatef[:, 0:1], i, axis=AX.X)),
                   waits=[(pe, g_pe)])
            b.emit("vector",
                   (lambda e, i=gbank[:, 0:8]: e.tensor_scalar(
                       eq1v[:], i, gatef[:, 0:1], None, ALU.is_equal)))
            b.emit("vector",
                   (lambda e, i=gbank[:, 0:8]: e.scalar_tensor_tensor(
                       l2v[:], eq1v[:], -1e30, i, ALU.mult, ALU.add)))
            b.emit("vector",
                   lambda e: e.reduce_max(gatef[:, 1:2], l2v[:], axis=AX.X))
            b.emit("vector",
                   (lambda e: e.tensor_scalar(
                       eq2v[:], l2v[:], gatef[:, 1:2], None, ALU.is_equal)))
            b.emit("vector",
                   (lambda e: e.tensor_sub(gatef[:, 2:3], gatef[:, 0:1],
                                           gatef[:, 1:2])),
                   inc=dve)
            d_dve = dve.n
            b.emit("scalar",
                   (lambda e: e.activation(gatef[:, 3:4], gatef[:, 2:3],
                                           AF.Exp, scale=-1.0)),
                   waits=[(dve, d_dve)], inc=acte)
            e_act = acte.n
            b.emit("vector",
                   (lambda e: e.tensor_scalar(gatef[:, 4:5], gatef[:, 3:4],
                                              1.0, None, ALU.add)),
                   waits=[(acte, e_act)])
            b.emit("vector", lambda e: e.reciprocal(gatef[:, 5:6], gatef[:, 4:5]))
            b.emit("vector",
                   (lambda e: e.tensor_scalar(gatef[:, 4:5], gatef[:, 5:6],
                                              2.0, None, ALU.mult)))
            b.emit("vector",
                   (lambda e: e.tensor_scalar(gatef[:, 5:6], gatef[:, 4:5],
                                              -1.0, 2.0, ALU.mult, ALU.add)))
            b.emit("vector",
                   (lambda e: e.tensor_scalar(eq1v[:], eq1v[:], gatef[:, 4:5],
                                              None, ALU.mult)))
            b.emit("vector",
                   (lambda e: e.tensor_scalar(eq2v[:], eq2v[:], gatef[:, 5:6],
                                              None, ALU.mult)))
            b.emit("vector",
                   (lambda e, o=wcatf[:, tt * 8:(tt + 1) * 8]:
                    e.tensor_add(o, eq1v[:], eq2v[:])),
                   inc=dve)
            gate_prev = (dve, dve.n)
        GATE = dve.n

        b.emit("tensor",
               lambda e: e.transpose(pb[7][:, 0:128], wcatf[:], ident[:]),
               waits=[(dve, GATE), D_LD], inc=pe)
        TR = pe.n
        b.emit("scalar", lambda e: e.copy(wcatTb[:], pb[7][:, 0:128]),
               waits=[(pe, TR)], inc=acte)
        WCT = acte.n
        for chunk in range(4):
            for q in range(4):
                tt = chunk * 4 + q
                b.emit("tensor",
                       (lambda e, o=pb[4][:, q * 128:(q + 1) * 128],
                        l=rall[:, tt * 128:(tt + 1) * 128], s=(q == 0),
                        p2=(q == 3):
                        e.matmul(o, l, wcatTb[:], start=s, stop=p2)),
                       waits=([(acte, WCT)] if tt == 0 else
                              ([(acte, acte.n)] if q == 0 else ())),
                       inc=(pe if q == 3 else None))
            wx_pe = pe.n
            b.emit("scalar",
                   (lambda e, o=wexpb[:, chunk * TCW:(chunk + 1) * TCW]:
                    e.copy(o, pb[4][:, :])),
                   waits=[(pe, wx_pe)], inc=acte)
        WEXP = acte.n

        for tc in range(NTC):
            for ct in range(CT):
                b.emit("tensor",
                       (lambda e, o=pb[tc][:, :],
                        l=acatS[:, ct * 128:(ct + 1) * 128],
                        r=big1[:, ct * 2048 + tc * TCW:ct * 2048 + (tc + 1) * TCW],
                        s=(ct == 0), p2=(ct == CT - 1):
                        e.matmul(o, l, r, start=s, stop=p2)),
                       waits=([D_LD] if (tc == 0 and ct == 0) else ()),
                       inc=(pe if ct == CT - 1 else None))
            low_pe = pe.n
            b.emit("scalar",
                   (lambda e, o=lowTb[:, tc * TCW:(tc + 1) * TCW],
                    i=pb[tc][:, :]: e.copy(o, i)),
                   waits=[(pe, low_pe)], inc=acte)
        LOWA = acte.n
        b.emit("vector", lambda e: e.tensor_mul(tmpT[:], lowTb, wexpb),
               waits=[(acte, LOWA), (acte, WEXP)], inc=dve)
        TMPD = dve.n

        b.emit("sync", lambda e: e.nop(),
               waits=dma_totals() + [(dve, TMPD), (acte, acte.n), (pe, pe.n)],
               inc=bar)
        x1b_rhs = lambda ct, tc: big1[:, ct * 2048 + tc * TCW:ct * 2048 + (tc + 1) * TCW]
        r2_list = [r2s[i] for i in range(CT)]
        proj_phase(bw, x1b_rhs, x1_list, r2_list, True)

        if stop_after == "D1":
            early_stop()
            return nc, b.nops
        b.emit("sync", lambda e: e.nop(),
               waits=dma_totals() + [(dve, dve.n), (pe, pe.n), (acte, acte.n)],
               inc=bar)
        out_list = [outm[i] for i in range(CT)]
        apply_phase(r2_list, out_list, False)

        b.emit("sync", lambda e: e.nop(),
               waits=dma_totals() + [(dve, dve.n), (acte, acte.n), (pe, pe.n)],
               inc=bar)

        b.build_block()
    return nc, b.nops


def _w_tile(w):
    t = w.reshape(32, 128, 2, 16, 128)
    return np.ascontiguousarray(t.transpose(0, 2, 4, 3, 1))


def _host_tables():
    pos = np.arange(T, dtype=np.float32)
    thetas = (1.0 / 10000.0 ** (np.arange(0, HD, 2, dtype=np.float32) / HD))
    pt = np.concatenate([np.outer(pos, thetas)] * 2, axis=1)
    sin_t = np.sin(pt).astype(np.float32).T
    cos_t = np.cos(pt).astype(np.float32).T
    sgn = np.concatenate([-np.ones(64, np.float32), np.ones(64, np.float32)])
    return cos_t.astype(BFDT), (sin_t * (-sgn[:, None])).astype(BFDT)


def kernel(L0_attn_x, L0_attn_wq, L0_attn_wk, L0_attn_wv, L0_attn_wo,
           L0_ffn_gate_w, L0_ffn_base_w, L0_ffn_base_b, L0_ffn_lora_A,
           L0_ffn_lora_B):
    x = np.asarray(L0_attn_x, dtype=np.float32)
    wq_t = _w_tile(np.asarray(L0_attn_wq, np.float32))
    wk_t = _w_tile(np.asarray(L0_attn_wk, np.float32))
    wv_t = _w_tile(np.asarray(L0_attn_wv, np.float32))
    wo_t = _w_tile(np.asarray(L0_attn_wo, np.float32))
    bw_t = _w_tile(np.asarray(L0_ffn_base_w, np.float32))
    gw = np.asarray(L0_ffn_gate_w, np.float32)
    gwm = np.ascontiguousarray(
        gw.T.reshape(32, 128, 8).transpose(1, 0, 2).reshape(128, 256))
    A_cat = np.asarray(L0_ffn_lora_A, np.float32).reshape(128, 4096)
    acm = np.ascontiguousarray(
        A_cat.T.reshape(32, 128, 128).transpose(1, 0, 2).reshape(128, 4096))
    bcm = np.ascontiguousarray(
        np.asarray(L0_ffn_lora_B, np.float32).transpose(0, 2, 1).reshape(128, 4096))
    bbm = np.ascontiguousarray(
        np.asarray(L0_ffn_base_b, np.float32).reshape(32, 128).T)
    cos_t, sin_t = _host_tables()
    maskm = np.triu(np.ones((128, 128), np.float32)).astype(BFDT)
    rallm = np.zeros((128, 2048), np.float32)
    for tt in range(16):
        for er in range(128):
            rallm[tt * 8 + er // 16, tt * 128 + er] = 1.0
    rallm = rallm.astype(BFDT)
    idm = np.eye(128, dtype=np.float32)

    shared = dict(wq=wq_t, wk=wk_t, wv=wv_t, wo=wo_t, bw=bw_t, gwm=gwm,
                  acm=acm, bcm=bcm, bbm=bbm, cost=cos_t, sint=sin_t,
                  maskm=maskm, rallm=rallm, idm=idm)
    in_maps = []
    for bi in range(NCORES):
        xtc = np.ascontiguousarray(x[bi].T).reshape(CT, 128, T)
        in_maps.append(dict(shared, xt=xtc))

    nc, _ = build_program()
    res = run_bass_kernel_spmd(nc, in_maps, core_ids=list(range(NCORES)))
    out = np.empty((NCORES, T, C), dtype=np.float32)
    for bi in range(NCORES):
        o = np.asarray(res.results[bi]["outm"], np.float32)
        out[bi] = o.reshape(C, T).T
    return out


if __name__ == "__main__":
    import time
    t0 = time.time()
    nc, nops = build_program()
    print(f"built in {time.time() - t0:.1f}s; ops per engine: {nops}")


# version 04 (final)
# speedup vs baseline: 1.2078x; 1.2078x over previous
import math
import contextlib
import numpy as np
import ml_dtypes
import concourse.bass as bass
import concourse.mybir as mybir
from concourse.bass_utils import run_bass_kernel_spmd

F32 = mybir.dt.float32
BF16 = mybir.dt.bfloat16
AF = mybir.ActivationFunctionType
ALU = mybir.AluOpType
AX = mybir.AxisListType
BFDT = ml_dtypes.bfloat16

NH, HD, T, C, CT = 32, 128, 2048, 4096, 32
NTC, TCW = 4, 512
S128I = 1.0 / math.sqrt(128.0)
LNS128 = -math.log(math.sqrt(128.0))
NCORES = 8


class Sem:
    __slots__ = ("h", "n")

    def __init__(self, h):
        self.h = h
        self.n = 0


class Builder:
    ENGINES = ("tensor", "vector", "scalar", "gpsimd", "sync")

    def __init__(self, nc, es):
        self.nc = nc
        self.es = es
        self.ops = []
        self.sems = {}
        self.self_sems = {}
        self.nops = {e: 0 for e in self.ENGINES}

    def sem(self, name):
        if name not in self.sems:
            h = self.es.enter_context(self.nc.semaphore(name))
            self.sems[name] = Sem(h)
        return self.sems[name]

    @staticmethod
    def _flat(waits):
        out = []
        for w in waits:
            if w is None:
                continue
            if isinstance(w, list):
                out.extend(c for c in w if c is not None and c[1] > 0)
            elif w[1] > 0:
                out.append(w)
        return out

    def emit(self, eng, fn, waits=(), inc=None, inc_amt=1):
        waits = self._flat(waits)
        self_sem = self.self_sems.get(eng)
        if self_sem is not None:
            if self_sem.n > 0:
                waits = waits + [(self_sem, self_sem.n)]
            assert inc is None or inc is self_sem
            inc = self_sem
        incs = []
        if inc is not None:
            inc.n += inc_amt
            incs = [(inc, inc_amt)]
        self.ops.append((eng, fn, waits, incs))
        self.nops[eng] += 1

    def dma(self, queue, out_ap, in_ap, waits=(), inc=None, **kw):
        self.emit(queue, lambda e: e.dma_start(out_ap, in_ap, **kw),
                  waits, inc, inc_amt=16)

    def build_block(self):
        nc = self.nc
        with nc.Block() as block:
            for eng_name in self.ENGINES:
                ops = [o for o in self.ops if o[0] == eng_name]
                if not ops:
                    continue

                def body(e, ops=ops):
                    lw = {}
                    for (_, fn, waits, incs) in ops:
                        for (s, v) in waits:
                            if lw.get(id(s), 0) < v:
                                e.wait_ge(s.h, v)
                                lw[id(s)] = v
                        instr = fn(e)
                        for (s, amt) in incs:
                            instr.then_inc(s.h, amt)

                getattr(block, eng_name)(body)


def build_program(stop_after=None):
    nc = bass.Bass("TRN2", target_bir_lowering=False)
    dt = nc.dram_tensor

    xt = dt("xt", [CT, 128, T], F32, kind="ExternalInput").ap()
    wq = dt("wq", [NH, 2, 128, 16, 128], F32, kind="ExternalInput").ap()
    wk = dt("wk", [NH, 2, 128, 16, 128], F32, kind="ExternalInput").ap()
    wv = dt("wv", [NH, 2, 128, 16, 128], F32, kind="ExternalInput").ap()
    wo = dt("wo", [NH, 2, 128, 16, 128], F32, kind="ExternalInput").ap()
    bw = dt("bw", [NH, 2, 128, 16, 128], F32, kind="ExternalInput").ap()
    gwm = dt("gwm", [128, 256], F32, kind="ExternalInput").ap()
    acm = dt("acm", [128, 4096], F32, kind="ExternalInput").ap()
    bcm = dt("bcm", [128, 4096], F32, kind="ExternalInput").ap()
    bbm = dt("bbm", [128, 32], F32, kind="ExternalInput").ap()
    cost = dt("cost", [128, T], BF16, kind="ExternalInput").ap()
    sint = dt("sint", [128, T], BF16, kind="ExternalInput").ap()
    maskm = dt("maskm", [128, 128], BF16, kind="ExternalInput").ap()
    rallm = dt("rallm", [128, 2048], BF16, kind="ExternalInput").ap()
    idm = dt("idm", [128, 128], F32, kind="ExternalInput").ap()
    outm = dt("outm", [CT, 128, T], F32, kind="ExternalOutput").ap()

    qs = dt("qs", [NH, 128, T], BF16, kind="Internal").ap()
    ks = dt("ks", [NH, 128, T], BF16, kind="Internal").ap()
    vs = dt("vs", [NH, 128, T], BF16, kind="Internal").ap()
    r1s = dt("r1s", [CT, 128, T], F32, kind="Internal").ap()
    x1s = dt("x1s", [CT, 128, T], F32, kind="Internal").ap()
    r2s = dt("r2s", [CT, 128, T], F32, kind="Internal").ap()

    es = contextlib.ExitStack()
    with es:
        E = es.enter_context
        st = nc.sbuf_tensor
        big1 = E(st("big1", [128, 65536], BF16))
        wring = E(st("wring", [128, 6144], BF16))
        tabs = E(st("tabs", [128, 4096], BF16))
        qkb = E(st("qkb", [128, 5120], BF16))
        vb3 = E(st("vb3", [128, 4096], BF16))
        preb = E(st("preb", [128, 4096], BF16))
        sqa = E(st("sqa", [128, 4096], BF16))
        f32a = E(st("f32a", [128, 3328], F32))
        rows = E(st("rows", [128, 2048], F32))
        bbs = E(st("bbs", [128, 32], F32))
        gws = E(st("gws", [128, 256], BF16))
        ident = E(st("ident", [128, 128], F32))
        maskb = E(st("maskb", [128, 128], BF16))
        onesf = E(st("onesf", [128, 256], F32))
        onesb = E(st("onesb", [128, 1], BF16))
        crow = E(st("crow", [128, 8], F32))
        pb = [E(nc.psum_tensor(f"pb{i}", [128, 512], F32)) for i in range(8)]

        b = Builder(nc, es)

        wslot = lambda s: wring[:, s * 2048:(s + 1) * 2048]
        costab = tabs[:, 0:2048]
        sintab = tabs[:, 2048:4096]
        qrope, krope = qkb[:, 0:2048], qkb[:, 2048:4096]
        vbA = lambda p_: vb3[:, p_ * 2048:(p_ + 1) * 2048]
        pre = lambda p_: preb[:, p_ * 2048:(p_ + 1) * 2048]
        sqbuf = sqa[:, 0:2048]
        ropetmp = sqa[:, 2048:4096]
        qparts = rows[64:65, 1568:1696]
        qm2row = rows[64:65, 1536:1568]
        knm2row = rows[64:65, 0:512]
        b2row = rows[64:65, 512:1024]
        lntmp = rows[64:65, 1024:1536]
        zrecr = rows[64:65, 1536:2048]
        browf = rows[64:65, 0:512]
        arow, brow = rows[0:1, 0:2048], rows[32:33, 0:2048]
        krsB = lambda p_: qkb[:, p_ * 2048:(p_ + 1) * 2048]
        qchB = lambda p_: qkb[:, 4096 + p_ * 512:4096 + (p_ + 1) * 512]
        vtok = lambda p_: vb3[:, p_ * 2048:(p_ + 1) * 2048]
        _attn_off = (0, 512, 1024, 2048)
        attnS = lambda s: sqa[:, _attn_off[s]:_attn_off[s] + 512]
        ebB = sqa[:, 1536:2048]
        negb = f32a[:, 0:512]
        ebf = f32a[:, 512:1024]
        zbc = f32a[:, 1024:1536]
        xres = lambda p_: f32a[:, p_ * 512:(p_ + 1) * 512]
        rtile = lambda p_: f32a[:, 1024 + p_ * 512:1024 + (p_ + 1) * 512]
        rload = lambda p_: f32a[:, p_ * 512:(p_ + 1) * 512]
        abc, bbc = f32a[:, 1024:1536], f32a[:, 1536:2048]
        outt = lambda p_: f32a[:, 2048 + p_ * 512:2048 + (p_ + 1) * 512]
        gatef = f32a[:, 3072:3104]
        eq1v, eq2v, l2v = f32a[:, 3104:3112], f32a[:, 3112:3120], f32a[:, 3120:3128]
        wcatf = f32a[:, 3200:3328]
        rbB = sqa[:, 0:512]
        sqB = sqa[:, 512:1024]
        wcatTb = sqa[:, 1280:1408]
        rall = sqa[:, 2048:4096]
        lowTb = preb[:, 0:2048]
        tmpT = preb[:, 2048:4096]
        wexpb = tabs[:, 0:2048]
        acatS = vb3[:, 0:4096]
        bcatS = qkb[:, 0:4096]

        pe = b.sem("pe")
        acte = b.sem("acte")
        dve = b.sem("dve")
        bar = b.sem("bar")
        b.self_sems["vector"] = dve
        b.self_sems["scalar"] = acte
        gconst = b.sem("gconst")
        gx = b.sem("gx")
        gw3 = [b.sem(f"gw{i}") for i in range(3)]
        gk2 = [b.sem(f"gk{i}") for i in range(2)]
        gq2 = [b.sem(f"gq{i}") for i in range(2)]
        gxr2 = [b.sem(f"gxr{i}") for i in range(2)]
        grl2 = [b.sem(f"grl{i}") for i in range(2)]
        gd = b.sem("gd")
        sqs = b.sem("sqs")
        sks = b.sem("sks")
        sv2 = [b.sem(f"sv{i}") for i in range(2)]
        vt2 = [b.sem(f"vt{i}") for i in range(2)]
        sr2 = [b.sem(f"sr{i}") for i in range(2)]
        so2 = [b.sem(f"so{i}") for i in range(2)]
        ALL_DMA = [gconst, gx, gd, sqs, sks] + gw3 + gk2 + gq2 + gxr2 + grl2 + sv2 + vt2 + sr2 + so2

        def dma_totals():
            return [(sm, sm.n) for sm in ALL_DMA]

        for (dst, src) in ((costab, cost[:]), (sintab, sint[:]),
                           (maskb[:], maskm[:]), (ident[:], idm[:]),
                           (bbs[:], bbm[:]), (gws[:], gwm[:])):
            b.dma("gpsimd", dst, src, inc=gconst)
        CONSTS = (gconst, gconst.n)
        b.emit("vector", lambda e: e.memset(onesf[:, 0:128], 1.0))
        b.emit("vector", lambda e: e.memset(onesf[:, 128:256], -1.0))
        b.emit("vector", lambda e: e.memset(crow[64:65, 0:1], LNS128))
        b.emit("vector", lambda e: e.memset(crow[64:65, 1:2], 1e-5))
        b.emit("vector", lambda e: e.memset(onesb[:], 1.0), inc=dve)
        CONST_DVE = dve.n


        def early_stop():
            b.dma("sync", outm[0], rows[:, :], inc=so2[0])
            b.emit("sync", lambda e: e.nop(),
                   waits=dma_totals() + [(dve, dve.n), (acte, acte.n),
                                         (pe, pe.n)], inc=bar)
            b.build_block()
        for ct in range(CT):
            b.dma("gpsimd", big1[:, ct * 2048:(ct + 1) * 2048], xt[ct], inc=gx)
        XLD = (gx, gx.n)

        wsrc = ("q", "k", "v")
        wmap = {"q": wq, "k": wk, "v": wv}
        half_gld = {}
        slot_free = {0: None, 1: None, 2: None}
        m_next = 0

        def emit_half_load(m):
            nonlocal m_next
            p_, hf = divmod(m, 2)
            h, wi = divmod(p_, 3)
            sm = gw3[m % 3]
            b.dma("gpsimd", wslot(m % 3), wmap[wsrc[wi]][h, hf],
                  waits=[slot_free[m % 3]], inc=sm)
            half_gld[m] = (sm, sm.n)
            m_next = m + 1

        NPROJ = 3 * NH
        skip_bprep = False
        if stop_after and stop_after.startswith("A:"):
            NPROJ = int(stop_after.split(":")[1])
            skip_bprep = True
        if NPROJ >= 0:
            for m in range(3):
                emit_half_load(m)

        group_free = {0: [], 1: []}
        spill_q = None
        spill_k = None
        vb_free = {0: None, 1: None}
        pre_free = {0: None, 1: None}
        penorm = b.sem("penorm")
        pending_norm = []

        for p in range(max(0, NPROJ)):
            h, wi = divmod(p, 3)
            wname = wsrc[wi]
            G = p % 2
            banks = pb[4 * G:4 * G + 4]
            gf = []
            for it in group_free[G]:
                if isinstance(it, tuple) and it[0] == "norm_red":
                    gf.extend(it[1])
                else:
                    gf.append(it)
            waits0 = [half_gld[2 * p], half_gld[2 * p + 1], XLD] + gf
            for ct in range(CT):
                hf, cth = divmod(ct, 16)
                lhsT = wslot((2 * p + hf) % 3)[:, cth * 128:(cth + 1) * 128]
                for tc in range(NTC):
                    rhs = big1[:, ct * 2048 + tc * TCW:ct * 2048 + (tc + 1) * TCW]
                    last = (ct == CT - 1)
                    b.emit("tensor",
                           (lambda e, o=banks[tc][:, :], l=lhsT, r=rhs,
                            s=(ct == 0), p2=last:
                            e.matmul(o, l, r, start=s, stop=p2)),
                           waits=(waits0 if (ct == 0 and tc == 0) else ()),
                           inc=(pe if (last and tc == NTC - 1) else None))
            proj_done = pe.n
            slot_free[(2 * p) % 3] = (pe, proj_done)
            slot_free[(2 * p + 1) % 3] = (pe, proj_done)
            if m_next < 192:
                emit_half_load(m_next)
            if m_next < 192:
                emit_half_load(m_next)
            while pending_norm:
                pending_norm.pop(0)()

            if wname == "v":
                dest = vbA(h % 2)
                evw = [(pe, proj_done), vb_free[h % 2]]
            else:
                dest = pre(p % 2)
                evw = [(pe, proj_done), pre_free[p % 2]]
            for tc in range(NTC):
                b.emit("scalar",
                       (lambda e, o=dest[:, tc * TCW:(tc + 1) * TCW],
                        i=banks[tc][:, :]: e.copy(o, i)),
                       waits=(evw if tc == 0 else ()),
                       inc=(acte if tc == NTC - 1 else None))
            evac_done = acte.n

            if wname == "v":
                svm = sv2[h % 2]
                b.dma("sync", vs[h], dest, waits=[(acte, evac_done)], inc=svm)
                vb_free[h % 2] = (svm, svm.n)
                group_free[G] = [(acte, evac_done)]
            else:
                b.emit("vector",
                       (lambda e, d=dest: e.tensor_mul(sqbuf[:], d, d)),
                       waits=[(acte, evac_done), (penorm, penorm.n + 4 * len(pending_norm))],
                       inc=dve)
                sq_done = dve.n
                red_box = []
                def _emit_norms(banks=banks, sq_done=sq_done, wname=wname, h=h,
                                red_box=red_box):
                    for tc in range(NTC):
                        nb = banks[tc]
                        w = [(dve, sq_done)]
                        b.emit("tensor",
                               (lambda e, o=nb[0:1, :],
                                r=sqbuf[:, tc * TCW:(tc + 1) * TCW]:
                                e.matmul(o, onesb[:], r, start=True, stop=True)),
                               waits=w, inc=penorm)
                        mm_done = penorm.n
                        if wname == "q":
                            b.emit("vector",
                                   (lambda e, o=qparts[0:1, h * 4 + tc:h * 4 + tc + 1],
                                    i=nb[0:1, :]: e.reduce_max(o, i, axis=AX.X)),
                                   waits=[(penorm, mm_done)], inc=dve)
                        else:
                            b.emit("vector",
                                   (lambda e,
                                    o=knm2row[0:1, h * 16 + tc * 4:h * 16 + tc * 4 + 4],
                                    i=nb[0:1, :].rearrange("p (a x) -> p a x", a=4):
                                    e.reduce_max(o, i, axis=AX.X)),
                                   waits=[(penorm, mm_done)], inc=dve)
                    red_box.append((dve, dve.n))
                pending_norm.append(_emit_norms)
                norms_done = (penorm, penorm.n + 4 * len(pending_norm))
                rope_out = qrope if wname == "q" else krope
                rope_war = spill_q if wname == "q" else spill_k
                b.emit("vector",
                       (lambda e, o=rope_out, d=dest: e.tensor_mul(o, d, costab)),
                       waits=[rope_war, CONSTS])
                b.emit("vector",
                       (lambda e, d=dest: e.tensor_mul(ropetmp[:], d, sintab)))
                b.emit("vector",
                       (lambda e: e.tensor_copy(sqbuf[0:64, :], ropetmp[64:128, :])),
                       waits=[norms_done])
                b.emit("vector",
                       (lambda e: e.tensor_copy(sqbuf[64:128, :], ropetmp[0:64, :])))
                b.emit("vector",
                       (lambda e, o=rope_out: e.tensor_add(o, o, sqbuf[:])),
                       inc=dve)
                rope_done = dve.n
                pre_free[p % 2] = (dve, rope_done)
                ssm = sqs if wname == "q" else sks
                b.dma("sync", (qs if wname == "q" else ks)[h], rope_out,
                      waits=[(dve, rope_done)], inc=ssm)
                if wname == "q":
                    spill_q = (ssm, ssm.n)
                else:
                    spill_k = (ssm, ssm.n)
                group_free[G] = [(acte, evac_done), ("norm_red", red_box)]

        while pending_norm:
            pending_norm.pop(0)()
        if skip_bprep:
            early_stop()
            return nc, b.nops
        b.emit("vector",
               (lambda e: e.reduce_max(
                   qm2row[0:1, 0:32],
                   qparts.rearrange("p (h f) -> p h f", f=4), axis=AX.X)))
        for h in range(NH):
            b.emit("vector",
                   (lambda e, o=b2row[0:1, h * 16:(h + 1) * 16],
                    i=knm2row[0:1, h * 16:(h + 1) * 16],
                    s=qm2row[0:1, h:h + 1]:
                    e.tensor_scalar(o, i, s, None, ALU.mult)),
                   inc=(dve if h == NH - 1 else None))
        BPREP = dve.n
        if stop_after == "AB1":
            early_stop()
            return nc, b.nops
        b.emit("scalar",
               lambda e: e.activation(lntmp[0:1, :], b2row[0:1, :], AF.Ln),
               waits=[(dve, BPREP)])
        b.emit("scalar",
               lambda e: e.activation(browf[0:1, :], lntmp[0:1, :], AF.Exp,
                                      scale=0.5, bias=crow[64:65, 0:1]),
               inc=acte)
        BROW = acte.n
        if stop_after == "AB2":
            early_stop()
            return nc, b.nops
        b.emit("tensor",
               lambda e: e.matmul(pb[0][:, :], onesf[64:65, 128:256], browf[0:1, :],
                                  start=True, stop=True),
               waits=[(acte, BROW), (dve, CONST_DVE)], inc=pe)
        BB_PE = pe.n
        if stop_after == "AB2a":
            early_stop()
            return nc, b.nops
        b.emit("scalar", lambda e: e.copy(negb[:], pb[0][:, :]),
               waits=[(pe, BB_PE)])
        b.emit("scalar", lambda e: e.activation(ebf[:], pb[0][:, :], AF.Exp,
                                                scale=-1.0),
               waits=[(pe, BB_PE)])
        EB_ACT = acte.n
        if stop_after == "AB2c":
            early_stop()
            return nc, b.nops
        b.emit("vector", lambda e: e.tensor_copy(ebB[:], ebf[:]),
               waits=[(acte, EB_ACT), (sqs, sqs.n), (sks, sks.n)], inc=dve)
        EBB = dve.n
        if stop_after == "AB3":
            early_stop()
            return nc, b.nops

        b.emit("sync", lambda e: e.nop(),
               waits=dma_totals() + [(dve, dve.n), (acte, acte.n), (pe, pe.n)],
               inc=bar)
        BAR_A = bar.n
        if stop_after == "A":
            early_stop()
            return nc, b.nops

        pesc = b.sem("pesc")
        peav = b.sem("peav")
        poolm = b.sem("poolm")
        exp_vals = []
        pending_fin = []
        peav_expect = [0]
        zrow_pe = [0]
        zbc_dve = [0]
        fin_val = [0, 0]
        krs_free = {0: None, 1: None}
        qch_free = {0: None, 1: None}
        vtok_free = {0: None, 1: None}
        attn_free = [None, None, None, None]
        blk_i = 0

        for h in range(NH):
            par = h % 2
            b.dma("gpsimd", krsB(par), ks[h],
                  waits=[(bar, BAR_A), krs_free[par]], inc=gk2[par])
            k_ld = (gk2[par], gk2[par].n)
            for j in range(16):
                b.dma("sync", vtok(par)[:, j * 128:(j + 1) * 128],
                      vs[h][:, j * 128:(j + 1) * 128],
                      waits=([(bar, BAR_A), vtok_free[par]] if j == 0 else ()),
                      inc=vt2[par], transpose=True)
            vt_all = (vt2[par], vt2[par].n)
            for j in range(16):
                b.emit("vector",
                       (lambda e, o=vtok(par)[:, j * 128:(j + 1) * 128],
                        s=ebf[:, h * 16 + j:h * 16 + j + 1]:
                        e.tensor_scalar(o, o, s, None, ALU.mult)),
                       waits=[vt_all, (acte, EB_ACT), (dve, EBB)])
            pres_done = dve.n

            for c in range(NTC):
                qpar = c % 2
                b.dma("gpsimd", qchB(qpar), qs[h][:, c * TCW:(c + 1) * TCW],
                      waits=[(bar, BAR_A), qch_free[qpar]], inc=gq2[qpar])
                q_ld = (gq2[qpar], gq2[qpar].n)
                J = 4 * c + 4
                par_c = (h * NTC + c) % 2
                avb = pb[2 if par_c == 0 else 5]
                zb_ = pb[3 if par_c == 0 else 6]
                pending_avz = []
                for j in range(J):
                    off = max(0, j * 128 - c * TCW)
                    scb = pb[(0, 1, 7)[blk_i % 3]]
                    slot = blk_i % 4
                    b.emit("tensor",
                           (lambda e, o=scb[:, off:TCW],
                            l=krsB(par)[:, j * 128:(j + 1) * 128],
                            r=qchB(qpar)[:, off:TCW]:
                            e.matmul(o, l, r, start=True, stop=True)),
                           waits=[k_ld, q_ld,
                                  ((acte, exp_vals[blk_i - 3])
                                   if blk_i >= 3 else None)],
                           inc=pesc)
                    sc_done = pesc.n
                    if j == 1 and pending_fin:
                        pending_fin.pop(0)()
                    b.emit("scalar",
                           (lambda e, o=attnS(slot)[:, off:TCW],
                            i=scb[:, off:TCW],
                            bia=negb[:, h * 16 + j:h * 16 + j + 1]:
                            e.activation(o, i, AF.Exp, bias=bia, scale=S128I)),
                           waits=[(pesc, sc_done), attn_free[slot]])
                    exp_done = acte.n
                    exp_vals.append(exp_done)
                    mask_done = None
                    if j >= 4 * c:
                        b.emit("gpsimd",
                               (lambda e, o=attnS(slot)[:, off:off + 128]:
                                e.tensor_mul(o, o, maskb[:])),
                               waits=[(acte, exp_done)], inc=poolm)
                        mask_done = (poolm, poolm.n)
                    def _avz(off=off, j=j, J=J, slot=slot, exp_done=exp_done,
                             mask_done=mask_done, avb=avb, zb_=zb_,
                             pres_done=pres_done, par=par, h=h,
                             par_c=par_c):
                        avw = [(acte, exp_done), (dve, pres_done), mask_done]
                        if j == 0:
                            avw.append((dve, fin_val[par_c]))
                        b.emit("tensor",
                               (lambda e, o=avb[:, off:TCW],
                                l=vtok(par)[:, j * 128:(j + 1) * 128],
                                r=attnS(slot)[:, off:TCW], s=(j == 0),
                                p2=(j == J - 1):
                                e.matmul(o, l, r, start=s, stop=p2)),
                               waits=avw)
                        b.emit("tensor",
                               (lambda e, o=zb_[0:1, off:TCW],
                                l=ebB[:, h * 16 + j:h * 16 + j + 1],
                                r=attnS(slot)[:, off:TCW], s=(j == 0),
                                p2=(j == J - 1):
                                e.matmul(o, l, r, start=s, stop=p2)),
                               inc=peav)
                    pending_avz.append(_avz)
                    peav_expect[0] += 1
                    attn_free[slot] = (peav, peav_expect[0])
                    if len(pending_avz) > 2:
                        pending_avz.pop(0)()
                    blk_i += 1
                while pending_avz:
                    pending_avz.pop(0)()
                avz_done = peav.n
                def _fin(avb=avb, zb_=zb_, avz_done=avz_done, h=h, c=c,
                         par_c=par_c):
                    b.emit("scalar",
                           (lambda e, i=zb_[0:1, :]: e.copy(zrecr[0:1, :], i)),
                           waits=[(peav, avz_done), (pe, zrow_pe[0])])
                    zrow_act = acte.n
                    b.emit("tensor",
                           (lambda e: e.matmul(pb[4][:, :], onesf[64:65, 0:128],
                                               zrecr[0:1, :],
                                               start=True, stop=True)),
                           waits=[(acte, zrow_act), (dve, zbc_dve[0])], inc=pe)
                    zrow_pe[0] = pe.n
                    b.emit("vector",
                           (lambda e: e.reciprocal(zbc[:], pb[4][:, :])),
                           waits=[(pe, zrow_pe[0])])
                    b.emit("vector",
                           (lambda e, i=avb[:, :],
                            o=big1[:, h * 2048 + c * TCW:h * 2048 + (c + 1) * TCW]:
                            e.tensor_mul(o, i, zbc[:])))
                    zbc_dve[0] = dve.n
                    fin_val[par_c] = dve.n
                pending_fin.append(_fin)
                qch_free[qpar] = (pesc, pesc.n)
            krs_free[par] = (pesc, pesc.n)
            vtok_free[par] = (peav, peav.n)

        bank_free = {}

        def proj_phase(wt, rhs_of, resid_list, spill_list, with_bias_lora):
            half_gld2 = {}
            slot_free2 = {0: None, 1: None, 2: None}
            stat_rows_done = None
            xres_free = [None, None]
            rtile_free = [None, None]
            rb_free = None
            pending_stats = []
            pestat = b.sem("pestat")

            def load_half(m):
                obr = (m // 2) % 32
                hf = m % 2
                sm = gw3[m % 3]
                b.dma("gpsimd", wslot(m % 3), wt[obr, hf],
                      waits=[(bar, bar.n), slot_free2[m % 3]], inc=sm)
                half_gld2[m] = (sm, sm.n)

            for mm_ in range(3):
                load_half(mm_)
            m2 = 3
            for pair in range(2):
                tcs = (2 * pair, 2 * pair + 1)
                for ob in range(CT):
                    m = (pair * 32 + ob) * 2
                    last_tc_pe = None
                    for k_tc, tc in enumerate(tcs):
                        bank = pb[(0 if ob % 2 == 0 else 6) + k_tc]
                        w0 = [half_gld2[m], half_gld2[m + 1], (bar, bar.n),
                              bank_free.get((ob % 2, k_tc))]
                        for ct in range(CT):
                            hf, cth = divmod(ct, 16)
                            lhsT = wslot((m + hf) % 3)[:, cth * 128:(cth + 1) * 128]
                            rhs = rhs_of(ct, tc)
                            last = (ct == CT - 1 and not with_bias_lora)
                            b.emit("tensor",
                                   (lambda e, o=bank[:, :], l=lhsT, r=rhs,
                                    s=(ct == 0), p2=last:
                                    e.matmul(o, l, r, start=s, stop=p2)),
                                   waits=(w0 if ct == 0 else ()),
                                   inc=(pe if last else None))
                        if with_bias_lora:
                            b.emit("tensor",
                                   (lambda e, o=bank[:, :],
                                    l=bcatS[:, ob * 128:(ob + 1) * 128],
                                    r=tmpT[:, tc * TCW:(tc + 1) * TCW]:
                                    e.matmul(o, l, r, start=False, stop=True)),
                                   inc=pe)
                        proj_done = pe.n
                        last_tc_pe = proj_done
                        xr = ob % 2
                        b.dma("gpsimd", xres(xr),
                              resid_list[ob][:, tc * TCW:(tc + 1) * TCW],
                              waits=[xres_free[xr], (bar, bar.n)], inc=gxr2[xr])
                        xres_ld = (gxr2[xr], gxr2[xr].n)
                        rt = (ob * 2 + k_tc) % 2
                        b.emit("vector",
                               (lambda e, o=rtile(rt), i=bank[:, :],
                                x=xres(xr): e.tensor_add(o, i, x)),
                               waits=[(pe, proj_done), xres_ld,
                                      rtile_free[rt]],
                               inc=(None if with_bias_lora else dve))
                        if with_bias_lora:
                            b.emit("vector",
                                   (lambda e, o=rtile(rt), bia=bbs[:, ob:ob + 1]:
                                    e.tensor_scalar(o, o, bia, None, ALU.add)),
                                   inc=dve)
                        radd = dve.n
                        xres_free[xr] = (dve, radd)
                        bank_free[(ob % 2, k_tc)] = (dve, radd)
                        srm = sr2[rt]
                        b.dma("sync", spill_list[ob][:, tc * TCW:(tc + 1) * TCW],
                              rtile(rt), waits=[(dve, radd)], inc=srm)
                        spill_done = (srm, srm.n)
                        b.emit("scalar",
                               (lambda e, i=rtile(rt): e.copy(rbB[:], i)),
                               waits=[(dve, radd), rb_free], inc=acte)
                        rb_done = acte.n
                        rtile_free[rt] = [spill_done, (acte, rb_done)]
                        b.emit("vector",
                               (lambda e: e.tensor_mul(sqB[:], rbB[:], rbB[:])),
                               waits=[(acte, rb_done), rb_free], inc=dve)
                        sq_done = dve.n
                        def _emit_stats(k_tc=k_tc, ob=ob, rb_done=rb_done,
                                        sq_done=sq_done, srd=stat_rows_done):
                            sumw = [(acte, rb_done)]
                            if ob == 0:
                                sumw.append(srd)
                            b.emit("tensor",
                                   (lambda e, o=pb[2 + k_tc][0:1, :], s=(ob == 0),
                                    p2=(ob == CT - 1):
                                    e.matmul(o, onesb[:], rbB[:], start=s, stop=p2)),
                                   waits=sumw)
                            b.emit("tensor",
                                   (lambda e, o=pb[4 + k_tc][0:1, :], s=(ob == 0),
                                    p2=(ob == CT - 1):
                                    e.matmul(o, onesb[:], sqB[:], start=s, stop=p2)),
                                   waits=[(dve, sq_done)], inc=pestat)
                        pending_stats.append(_emit_stats)
                        rb_free = (pestat, pestat.n + len(pending_stats))
                    slot_free2[m % 3] = (pe, last_tc_pe)
                    while len(pending_stats) > 2:
                        pending_stats.pop(0)()
                    slot_free2[(m + 1) % 3] = (pe, last_tc_pe)
                    if m2 < 128:
                        load_half(m2)
                        m2 += 1
                    if m2 < 128:
                        load_half(m2)
                        m2 += 1
                while pending_stats:
                    pending_stats.pop(0)()
                stat_pe = pe.n
                stat_done = (pestat, pestat.n)
                rstd64 = knm2row
                for k_tc, tcx in enumerate(tcs):
                    tc = tcx
                    b.emit("vector",
                           (lambda e, i=pb[2 + k_tc][0:1, :]: e.tensor_scalar(
                               lntmp[0:1, :], i, 1.0 / C, None, ALU.mult)),
                           waits=[(pe, stat_pe), stat_done])
                    b.emit("vector",
                           (lambda e, i=pb[4 + k_tc][0:1, :]: e.tensor_scalar(
                               zrecr[0:1, :], i, 1.0 / C, None, ALU.mult)))
                    b.emit("vector",
                           (lambda e: e.tensor_mul(b2row[0:1, :], lntmp[0:1, :],
                                                   lntmp[0:1, :])))
                    b.emit("vector",
                           (lambda e: e.tensor_sub(zrecr[0:1, :], zrecr[0:1, :],
                                                   b2row[0:1, :])),
                           inc=dve)
                    var_done = dve.n
                    b.emit("scalar",
                           (lambda e: e.activation(b2row[0:1, :], zrecr[0:1, :],
                                                   AF.Ln, bias=crow[64:65, 1:2])),
                           waits=[(dve, var_done)])
                    b.emit("scalar",
                           (lambda e: e.activation(rstd64[0:1, :], b2row[0:1, :],
                                                   AF.Exp, scale=-0.5)),
                           inc=acte)
                    rstd = acte.n
                    b.emit("vector",
                           (lambda e: e.tensor_mul(b2row[0:1, :], lntmp[0:1, :],
                                                   rstd64[0:1, :])),
                           waits=[(acte, rstd)])
                    b.emit("vector",
                           (lambda e, o=arow[0:1, tc * TCW:(tc + 1) * TCW]:
                            e.tensor_copy(o, rstd64[0:1, :])))
                    b.emit("vector",
                           (lambda e, o=brow[0:1, tc * TCW:(tc + 1) * TCW]:
                            e.tensor_scalar(o, b2row[0:1, :], -1.0, None,
                                            ALU.mult)),
                           inc=dve)
                stat_rows_done = (dve, dve.n)

        def apply_phase(r_list, out_list, to_big1):
            rl_free = [None, None]
            ot_free = [None, None]
            for tc in range(NTC):
                b.emit("tensor",
                       (lambda e, r=arow[0:1, tc * TCW:(tc + 1) * TCW]:
                        e.matmul(pb[4][:, :], onesf[0:1, 0:128], r, start=True, stop=True)),
                       waits=[(bar, bar.n), (acte, acte.n), (dve, dve.n)])
                b.emit("tensor",
                       (lambda e, r=brow[0:1, tc * TCW:(tc + 1) * TCW]:
                        e.matmul(pb[5][:, :], onesf[32:33, 0:128], r, start=True, stop=True)),
                       inc=pe)
                bc_pe = pe.n
                b.emit("scalar", lambda e: e.copy(abc[:], pb[4][:, :]),
                       waits=[(pe, bc_pe)])
                b.emit("scalar", lambda e: e.copy(bbc[:], pb[5][:, :]), inc=acte)
                bc_act = acte.n
                for ob in range(CT):
                    rp = ob % 2
                    b.dma("gpsimd", rload(rp),
                          r_list[ob][:, tc * TCW:(tc + 1) * TCW],
                          waits=[rl_free[rp], (bar, bar.n)], inc=grl2[rp])
                    r_ld = (grl2[rp], grl2[rp].n)
                    op = ob % 2
                    b.emit("vector",
                           (lambda e, o=outt(op), r=rload(rp):
                            e.tensor_mul(o, r, abc[:])),
                           waits=[r_ld, (acte, bc_act), ot_free[op]])
                    b.emit("vector",
                           (lambda e, o=outt(op): e.tensor_add(o, o, bbc[:])),
                           inc=dve)
                    ap_done = dve.n
                    rl_free[rp] = (dve, ap_done)
                    som = so2[op]
                    b.dma("sync", out_list[ob][:, tc * TCW:(tc + 1) * TCW],
                          outt(op), waits=[(dve, ap_done)], inc=som)
                    conds = [(som, som.n)]
                    if to_big1:
                        b.emit("scalar",
                               (lambda e,
                                o=big1[:, ob * 2048 + tc * TCW:
                                       ob * 2048 + (tc + 1) * TCW],
                                i=outt(op): e.copy(o, i)),
                               waits=[(dve, ap_done)], inc=acte)
                        conds.append((acte, acte.n))
                    ot_free[op] = conds

        while pending_fin:
            pending_fin.pop(0)()
        if stop_after == "B":
            early_stop()
            return nc, b.nops
        b.emit("sync", lambda e: e.nop(),
               waits=dma_totals() + [(pe, pe.n), (pesc, pesc.n), (peav, peav.n),
                                     (acte, acte.n), (dve, dve.n)], inc=bar)
        avT_rhs = lambda ct, tc: big1[:, ct * 2048 + tc * TCW:ct * 2048 + (tc + 1) * TCW]
        xt_list = [xt[i] for i in range(CT)]
        r1_list = [r1s[i] for i in range(CT)]
        proj_phase(wo, avT_rhs, xt_list, r1_list, False)

        if stop_after == "C1":
            early_stop()
            return nc, b.nops
        b.emit("sync", lambda e: e.nop(),
               waits=dma_totals() + [(dve, dve.n), (pe, pe.n), (acte, acte.n)],
               inc=bar)
        x1_list = [x1s[i] for i in range(CT)]
        apply_phase(r1_list, x1_list, True)

        if stop_after == "C2":
            early_stop()
            return nc, b.nops
        b.emit("sync", lambda e: e.nop(),
               waits=dma_totals() + [(dve, dve.n), (pe, pe.n), (acte, acte.n)],
               inc=bar)
        BAR_C2 = bar.n
        b.dma("gpsimd", acatS, acm[:], waits=[(bar, BAR_C2)], inc=gd)
        b.dma("gpsimd", bcatS, bcm[:], waits=[(bar, BAR_C2)], inc=gd)
        b.dma("gpsimd", rall[:], rallm[:], waits=[(bar, BAR_C2)], inc=gd)
        D_LD = (gd, gd.n)

        gate_prev = None
        for tt in range(16):
            gbank = pb[6 + tt % 2]
            for ct in range(CT):
                b.emit("tensor",
                       (lambda e, o=gbank[:, 0:8],
                        l=big1[:, ct * 2048 + tt * 128:ct * 2048 + (tt + 1) * 128],
                        r=gws[:, ct * 8:(ct + 1) * 8], s=(ct == 0),
                        p2=(ct == CT - 1):
                        e.matmul(o, l, r, start=s, stop=p2)),
                       waits=([(bar, BAR_C2), gate_prev] if ct == 0 else ()),
                       inc=(pe if ct == CT - 1 else None))
            g_pe = pe.n
            b.emit("vector",
                   (lambda e, i=gbank[:, 0:8]: e.reduce_max(
                       gatef[:, 0:1], i, axis=AX.X)),
                   waits=[(pe, g_pe)])
            b.emit("vector",
                   (lambda e, i=gbank[:, 0:8]: e.tensor_scalar(
                       eq1v[:], i, gatef[:, 0:1], None, ALU.is_equal)))
            b.emit("vector",
                   (lambda e, i=gbank[:, 0:8]: e.scalar_tensor_tensor(
                       l2v[:], eq1v[:], -1e30, i, ALU.mult, ALU.add)))
            b.emit("vector",
                   lambda e: e.reduce_max(gatef[:, 1:2], l2v[:], axis=AX.X))
            b.emit("vector",
                   (lambda e: e.tensor_scalar(
                       eq2v[:], l2v[:], gatef[:, 1:2], None, ALU.is_equal)))
            b.emit("vector",
                   (lambda e: e.tensor_sub(gatef[:, 2:3], gatef[:, 0:1],
                                           gatef[:, 1:2])),
                   inc=dve)
            d_dve = dve.n
            b.emit("scalar",
                   (lambda e: e.activation(gatef[:, 3:4], gatef[:, 2:3],
                                           AF.Exp, scale=-1.0)),
                   waits=[(dve, d_dve)], inc=acte)
            e_act = acte.n
            b.emit("vector",
                   (lambda e: e.tensor_scalar(gatef[:, 4:5], gatef[:, 3:4],
                                              1.0, None, ALU.add)),
                   waits=[(acte, e_act)])
            b.emit("vector", lambda e: e.reciprocal(gatef[:, 5:6], gatef[:, 4:5]))
            b.emit("vector",
                   (lambda e: e.tensor_scalar(gatef[:, 4:5], gatef[:, 5:6],
                                              2.0, None, ALU.mult)))
            b.emit("vector",
                   (lambda e: e.tensor_scalar(gatef[:, 5:6], gatef[:, 4:5],
                                              -1.0, 2.0, ALU.mult, ALU.add)))
            b.emit("vector",
                   (lambda e: e.tensor_scalar(eq1v[:], eq1v[:], gatef[:, 4:5],
                                              None, ALU.mult)))
            b.emit("vector",
                   (lambda e: e.tensor_scalar(eq2v[:], eq2v[:], gatef[:, 5:6],
                                              None, ALU.mult)))
            b.emit("vector",
                   (lambda e, o=wcatf[:, tt * 8:(tt + 1) * 8]:
                    e.tensor_add(o, eq1v[:], eq2v[:])),
                   inc=dve)
            gate_prev = (dve, dve.n)
        GATE = dve.n

        b.emit("tensor",
               lambda e: e.transpose(pb[7][:, 0:128], wcatf[:], ident[:]),
               waits=[(dve, GATE), D_LD], inc=pe)
        TR = pe.n
        b.emit("scalar", lambda e: e.copy(wcatTb[:], pb[7][:, 0:128]),
               waits=[(pe, TR)], inc=acte)
        WCT = acte.n
        for chunk in range(4):
            for q in range(4):
                tt = chunk * 4 + q
                b.emit("tensor",
                       (lambda e, o=pb[4][:, q * 128:(q + 1) * 128],
                        l=rall[:, tt * 128:(tt + 1) * 128], s=(q == 0),
                        p2=(q == 3):
                        e.matmul(o, l, wcatTb[:], start=s, stop=p2)),
                       waits=([(acte, WCT)] if tt == 0 else
                              ([(acte, acte.n)] if q == 0 else ())),
                       inc=(pe if q == 3 else None))
            wx_pe = pe.n
            b.emit("scalar",
                   (lambda e, o=wexpb[:, chunk * TCW:(chunk + 1) * TCW]:
                    e.copy(o, pb[4][:, :])),
                   waits=[(pe, wx_pe)], inc=acte)
        WEXP = acte.n

        for tc in range(NTC):
            for ct in range(CT):
                b.emit("tensor",
                       (lambda e, o=pb[tc][:, :],
                        l=acatS[:, ct * 128:(ct + 1) * 128],
                        r=big1[:, ct * 2048 + tc * TCW:ct * 2048 + (tc + 1) * TCW],
                        s=(ct == 0), p2=(ct == CT - 1):
                        e.matmul(o, l, r, start=s, stop=p2)),
                       waits=([D_LD] if (tc == 0 and ct == 0) else ()),
                       inc=(pe if ct == CT - 1 else None))
            low_pe = pe.n
            b.emit("scalar",
                   (lambda e, o=lowTb[:, tc * TCW:(tc + 1) * TCW],
                    i=pb[tc][:, :]: e.copy(o, i)),
                   waits=[(pe, low_pe)], inc=acte)
        LOWA = acte.n
        b.emit("vector", lambda e: e.tensor_mul(tmpT[:], lowTb, wexpb),
               waits=[(acte, LOWA), (acte, WEXP)], inc=dve)
        TMPD = dve.n

        b.emit("sync", lambda e: e.nop(),
               waits=dma_totals() + [(dve, TMPD), (acte, acte.n), (pe, pe.n)],
               inc=bar)
        x1b_rhs = lambda ct, tc: big1[:, ct * 2048 + tc * TCW:ct * 2048 + (tc + 1) * TCW]
        r2_list = [r2s[i] for i in range(CT)]
        proj_phase(bw, x1b_rhs, x1_list, r2_list, True)

        if stop_after == "D1":
            early_stop()
            return nc, b.nops
        b.emit("sync", lambda e: e.nop(),
               waits=dma_totals() + [(dve, dve.n), (pe, pe.n), (acte, acte.n)],
               inc=bar)
        out_list = [outm[i] for i in range(CT)]
        apply_phase(r2_list, out_list, False)

        b.emit("sync", lambda e: e.nop(),
               waits=dma_totals() + [(dve, dve.n), (acte, acte.n), (pe, pe.n)],
               inc=bar)

        b.build_block()
    return nc, b.nops


def _w_tile(w):
    t = w.reshape(32, 128, 2, 16, 128)
    return np.ascontiguousarray(t.transpose(0, 2, 4, 3, 1))


def _host_tables():
    pos = np.arange(T, dtype=np.float32)
    thetas = (1.0 / 10000.0 ** (np.arange(0, HD, 2, dtype=np.float32) / HD))
    pt = np.concatenate([np.outer(pos, thetas)] * 2, axis=1)
    sin_t = np.sin(pt).astype(np.float32).T
    cos_t = np.cos(pt).astype(np.float32).T
    sgn = np.concatenate([-np.ones(64, np.float32), np.ones(64, np.float32)])
    return cos_t.astype(BFDT), (sin_t * (-sgn[:, None])).astype(BFDT)


def kernel(L0_attn_x, L0_attn_wq, L0_attn_wk, L0_attn_wv, L0_attn_wo,
           L0_ffn_gate_w, L0_ffn_base_w, L0_ffn_base_b, L0_ffn_lora_A,
           L0_ffn_lora_B):
    x = np.asarray(L0_attn_x, dtype=np.float32)
    wq_t = _w_tile(np.asarray(L0_attn_wq, np.float32))
    wk_t = _w_tile(np.asarray(L0_attn_wk, np.float32))
    wv_t = _w_tile(np.asarray(L0_attn_wv, np.float32))
    wo_t = _w_tile(np.asarray(L0_attn_wo, np.float32))
    bw_t = _w_tile(np.asarray(L0_ffn_base_w, np.float32))
    gw = np.asarray(L0_ffn_gate_w, np.float32)
    gwm = np.ascontiguousarray(
        gw.T.reshape(32, 128, 8).transpose(1, 0, 2).reshape(128, 256))
    A_cat = np.asarray(L0_ffn_lora_A, np.float32).reshape(128, 4096)
    acm = np.ascontiguousarray(
        A_cat.T.reshape(32, 128, 128).transpose(1, 0, 2).reshape(128, 4096))
    bcm = np.ascontiguousarray(
        np.asarray(L0_ffn_lora_B, np.float32).transpose(0, 2, 1).reshape(128, 4096))
    bbm = np.ascontiguousarray(
        np.asarray(L0_ffn_base_b, np.float32).reshape(32, 128).T)
    cos_t, sin_t = _host_tables()
    maskm = np.triu(np.ones((128, 128), np.float32)).astype(BFDT)
    rallm = np.zeros((128, 2048), np.float32)
    for tt in range(16):
        for er in range(128):
            rallm[tt * 8 + er // 16, tt * 128 + er] = 1.0
    rallm = rallm.astype(BFDT)
    idm = np.eye(128, dtype=np.float32)

    shared = dict(wq=wq_t, wk=wk_t, wv=wv_t, wo=wo_t, bw=bw_t, gwm=gwm,
                  acm=acm, bcm=bcm, bbm=bbm, cost=cos_t, sint=sin_t,
                  maskm=maskm, rallm=rallm, idm=idm)
    in_maps = []
    for bi in range(NCORES):
        xtc = np.ascontiguousarray(x[bi].T).reshape(CT, 128, T)
        in_maps.append(dict(shared, xt=xtc))

    nc, _ = build_program()
    res = run_bass_kernel_spmd(nc, in_maps, core_ids=list(range(NCORES)))
    out = np.empty((NCORES, T, C), dtype=np.float32)
    for bi in range(NCORES):
        o = np.asarray(res.results[bi]["outm"], np.float32)
        out[bi] = o.reshape(C, T).T
    return out


if __name__ == "__main__":
    import time
    t0 = time.time()
    nc, nops = build_program()
    print(f"built in {time.time() - t0:.1f}s; ops per engine: {nops}")
